# version 04
# baseline (speedup 1.0000x reference)
import numpy as np
import concourse.bass as bass
import concourse.mybir as mybir
from concourse.bass_utils import run_bass_kernel_spmd
from contextlib import ExitStack

F32 = mybir.dt.float32
BF16 = mybir.dt.bfloat16
U8 = mybir.dt.uint8
AF = mybir.ActivationFunctionType
ALU = mybir.AluOpType
AX = mybir.AxisListType.X

NCORES = 8
SPC = 2
T = 2048
D = 1024
TC = T // 128
DC = D // 128
H = 16
NEXP = 8
E3 = 6144
EMG = 2048
EC = EMG // 128
RS8 = 0.35355339059327373
STAGE = 3
MOE_ON = True
ATT_ON = True
DEBUG_TAPS = False
DEBUG_EC = 0

ENGS = ("tensor", "vector", "scalar", "gpsimd", "sync")


class Prog:

    def __init__(self, nc, es):
        self.nc = nc
        self.sems = {}
        for name in ("st", "sv", "ss", "dg", "dy", "we0", "we1", "qk0",
                     "qk1", "wk0", "wk1", "hs0", "hs1", "at0", "at1",
                     "u0", "u1", "ad0", "ad1", "wd0", "bc"):
            self.sems[name] = es.enter_context(nc.semaphore(name))
        self.cnt = {k: 0 for k in self.sems}
        self.dma_step = {k: 16 for k in self.sems
                         if k not in ("st", "sv", "ss")}
        self.wm = {e: {k: 0 for k in self.sems} for e in ENGS}
        self.cur = None
        self.n_inst = 0

    def begin(self):
        assert self.cur is None
        self.cur = {e: [] for e in ENGS}

    def end(self):
        q = self.cur
        self.cur = None
        with self.nc.Block() as blk:
            for e in ENGS:
                if q[e]:
                    def runner(eng, lst=q[e]):
                        for th in lst:
                            th(eng)
                    getattr(blk, e)(runner)
        for e in ENGS:
            for k in ("st", "sv", "ss"):
                self.wm[e][k] = max(self.wm[e][k], self.cnt[k])

    def need(self, eng, *tokens):
        for tok in tokens:
            if tok is None:
                continue
            k, c = tok
            if self.wm[eng][k] >= c:
                continue
            self.wm[eng][k] = c
            sem = self.sems[k]
            self.cur[eng].append(lambda E, sem=sem, c=c: E.wait_ge(sem, c))
            self.n_inst += 1

    def emit(self, eng, fn, inc=None):
        self.n_inst += 1
        if inc is not None:
            step = self.dma_step.get(inc, 1)
            self.cnt[inc] += step
            tok = (inc, self.cnt[inc])
            sem = self.sems[inc]

            def th(E, fn=fn, sem=sem, step=step):
                fn(E).then_inc(sem, step)
            self.cur[eng].append(th)
            return tok
        self.cur[eng].append(lambda E, fn=fn: fn(E))
        return None


def build_program():
    nc = bass.Bass("TRN2", target_bir_lowering=False)

    d_x = nc.dram_tensor("x", [SPC * T, D], F32, kind="ExternalInput").ap()
    d_W = nc.dram_tensor("W", [E3, D], F32, kind="ExternalInput").ap()
    d_b = nc.dram_tensor("b", [1, E3], F32, kind="ExternalInput").ap()
    d_Wd = nc.dram_tensor("Wd", [D, EMG], F32, kind="ExternalInput").ap()
    d_Wqkv = nc.dram_tensor("Wqkv", [D, 3 * D], F32, kind="ExternalInput").ap()
    d_bqkv = nc.dram_tensor("bqkv", [1, 3 * D], F32, kind="ExternalInput").ap()
    d_Wg = nc.dram_tensor("Wg", [D, NEXP], F32, kind="ExternalInput").ap()
    d_bg = nc.dram_tensor("bg", [1, NEXP], F32, kind="ExternalInput").ap()
    d_We = nc.dram_tensor("We", [NEXP, D, D], F32, kind="ExternalInput").ap()
    d_be = nc.dram_tensor("be", [NEXP, D], F32, kind="ExternalInput").ap()
    d_y = nc.dram_tensor("y", [SPC * T, D], F32, kind="ExternalOutput").ap()

    if DEBUG_TAPS:
        dbg_h = nc.dram_tensor("dbg_h", [EC * 128, T], BF16, kind="ExternalOutput").ap()
        dbg_w = nc.dram_tensor("dbg_w", [D, E3], BF16, kind="ExternalOutput").ap()
    sc_u = nc.dram_tensor("sc_u", [T, D], BF16, kind="Internal").ap()
    sc_att = nc.dram_tensor("sc_att", [D, T], BF16, kind="Internal").ap()
    sc_h = nc.dram_tensor("sc_h", [EC, 128, T], BF16, kind="Internal").ap()
    sc_Wbf = nc.dram_tensor("sc_Wbf", [E3, D], BF16, kind="Internal").ap()
    sc_WT = nc.dram_tensor("sc_WT", [D, E3], BF16, kind="Internal").ap()
    sc_Wdbf = nc.dram_tensor("sc_Wdbf", [D, EMG], BF16, kind="Internal").ap()
    sc_WdT = nc.dram_tensor("sc_WdT", [EMG, D], BF16, kind="Internal").ap()

    es = ExitStack()
    sbt = lambda n, s, dt: es.enter_context(nc.sbuf_tensor(n, s, dt))
    pst = lambda n, s: es.enter_context(nc.psum_tensor(n, s, F32))

    xres = sbt("xres", [128, TC * D], F32)
    uT = sbt("uT", [128, DC * T], BF16)
    u_tm = sbt("u_tm", [128, 2 * D], BF16)
    sx = sbt("sx", [128, TC], F32)
    sxx = sbt("sxx", [128, TC], F32)
    mcol = sbt("mcol", [128, TC], F32)
    ex2c = sbt("ex2c", [128, TC], F32)
    nvar = sbt("nvar", [128, TC], F32)
    lncol = sbt("lncol", [128, TC], F32)
    rstd = sbt("rstd", [128, TC], F32)
    r_bqkv = sbt("r_bqkv", [1, D], BF16)
    r_bg = sbt("r_bg", [1, NEXP], BF16)
    ones_r = sbt("ones_r", [1, 2048], BF16)
    ones_c = sbt("ones_c", [128, 1], BF16)
    c_eps = sbt("c_eps", [128, 1], F32)
    c_nh = sbt("c_nh", [128, 1], F32)
    bqkv_cs = sbt("bqkv_cs", [128, 16], F32)
    batt_cs = sbt("batt_cs", [128, 48], F32)
    bh05_cs = sbt("bh05_cs", [128, 16], F32)
    comb = sbt("comb", [128, 128], F32)

    PA = pst("PA", [128, 2048])
    PB = pst("PB", [128, 2048])

    p = Prog(nc, es)
    _uidc = [0]

    def _uid(n):
        _uidc[0] += 1
        return f"{n}_{_uidc[0]}"

    p0es = ExitStack()
    wstrip = p0es.enter_context(nc.sbuf_tensor("wstrip", [128, 8 * D], BF16))
    wtstrip = p0es.enter_context(nc.sbuf_tensor("wtstrip", [128, E3], BF16))

    p.begin()
    p.emit("gpsimd", lambda g: g.dma_start(r_bqkv[:, :],
                                           d_bqkv[0:1, 2 * D:3 * D]), inc="bc")
    p.emit("gpsimd", lambda g: g.dma_start(r_bg[:, :], d_bg), inc="bc")
    p.emit("vector", lambda v: v.memset(ones_r[0:1, :], 1.0))
    p.emit("vector", lambda v: v.memset(ones_c[:, 0:1], 1.0))
    p.emit("vector", lambda v: v.memset(c_eps[:, 0:1], 1e-5))
    p.emit("vector", lambda v: v.memset(c_nh[:, 0:1], -0.5))
    p.emit("gpsimd", lambda g: g.dma_start(
        bqkv_cs[:, :],
        d_bqkv[0:1, 0:2048].rearrange("o (c p) -> p (o c)", p=128),
        allow_slow_non_contiguous=True), inc="bc")
    tok_bc = p.emit("gpsimd", lambda g: g.dma_start(
        batt_cs[:, :],
        d_b[0:1, :].rearrange("o (c p) -> p (o c)", p=128),
        allow_slow_non_contiguous=True), inc="bc")
    p.need("vector", tok_bc)
    p.emit("vector", lambda v: v.tensor_scalar_mul(bqkv_cs[:, :],
                                                   bqkv_cs[:, :], RS8))
    tok_b5v = p.emit("vector", lambda v: v.tensor_scalar_add(
        bh05_cs[:, :], batt_cs[:, 32:48], 0.5), inc="sv")
    p.end()

    tok_wt_done = None
    tok_wdt_done = None
    if STAGE >= 2:
        p.begin()
        tok_store = None
        for i in range(6):
            if tok_store is not None:
                p.need("gpsimd", tok_store)
            tok_ld = p.emit("gpsimd", lambda g, i=i: g.dma_start(
                wstrip[:, :].rearrange("p (c d) -> p c d", c=8),
                d_W[i * 1024:(i + 1) * 1024, :].rearrange(
                    "(c p) d -> p c d", p=128)), inc="dg")
            p.need("sync", tok_ld)
            tok_store = p.emit("sync", lambda y, i=i: y.dma_start(
                sc_Wbf[i * 1024:(i + 1) * 1024, :].rearrange(
                    "(c p) d -> p c d", p=128),
                wstrip[:, :].rearrange("p (c d) -> p c d", c=8)), inc="dy")
        for dc in range(DC):
            p.need("sync", tok_store)
            tok_tr = p.emit("sync", lambda y, dc=dc: y.dma_start(
                wtstrip[:, :], sc_Wbf[:, dc * 128:(dc + 1) * 128],
                transpose=True), inc="dy")
            p.need("sync", tok_tr)
            tok_wt_done = p.emit("sync", lambda y, dc=dc: y.dma_start(
                sc_WT[dc * 128:(dc + 1) * 128, :],
                wtstrip[:, :]), inc="dy")
        for i in range(2):
            p.need("gpsimd", tok_store)
            tok_ld = p.emit("gpsimd", lambda g, i=i: g.dma_start(
                wstrip[:, 0:4 * EMG].rearrange("p (c e) -> p c e", c=4),
                d_Wd[i * 512:(i + 1) * 512, :].rearrange(
                    "(c p) e -> p c e", p=128)), inc="dg")
            p.need("sync", tok_ld)
            tok_store = p.emit("sync", lambda y, i=i: y.dma_start(
                sc_Wdbf[i * 512:(i + 1) * 512, :].rearrange(
                    "(c p) e -> p c e", p=128),
                wstrip[:, 0:4 * EMG].rearrange("p (c e) -> p c e", c=4)),
                inc="dy")
        for ec in range(EC):
            p.need("sync", tok_store, tok_wt_done)
            tok_tr = p.emit("sync", lambda y, ec=ec: y.dma_start(
                wtstrip[:, 0:D], sc_Wdbf[:, ec * 128:(ec + 1) * 128],
                transpose=True), inc="dy")
            p.need("sync", tok_tr)
            tok_wdt_done = p.emit("sync", lambda y, ec=ec: y.dma_start(
                sc_WdT[ec * 128:(ec + 1) * 128, :],
                wtstrip[:, 0:D]), inc="dy")
        p.end()
    p0es.close()

    def layernorm_to_uT():
        p.begin()
        for tc in range(TC):
            xt = xres[:, tc * D:(tc + 1) * D]
            p.emit("vector", lambda v, xt=xt, tc=tc:
                   v.reduce_sum(sx[:, tc:tc + 1], xt, axis=AX))
            p.emit("vector", lambda v, xt=xt, tc=tc:
                   v.scalar_tensor_tensor(uT[:, 0:D], xt, 1.0, xt,
                                          ALU.bypass, ALU.mult,
                                          accum_out=sxx[:, tc:tc + 1]))
        for tc in range(TC):
            p.emit("vector", lambda v, tc=tc:
                   v.tensor_scalar_mul(mcol[:, tc:tc + 1], sx[:, tc:tc + 1],
                                       1.0 / D))
        for tc in range(TC):
            p.emit("vector", lambda v, tc=tc:
                   v.tensor_scalar_mul(ex2c[:, tc:tc + 1], sxx[:, tc:tc + 1],
                                       1.0 / D))
        tok_nv = None
        for tc in range(TC):
            tok_nv = p.emit("vector", lambda v, tc=tc:
                            v.scalar_tensor_tensor(
                                nvar[:, tc:tc + 1], mcol[:, tc:tc + 1],
                                mcol[:, tc:tc + 1], ex2c[:, tc:tc + 1],
                                ALU.mult, ALU.subtract),
                            inc="sv" if tc == TC - 1 else None)
        p.need("scalar", tok_nv)
        for tc in range(TC):
            p.emit("scalar", lambda s, tc=tc:
                   s.activation(lncol[:, tc:tc + 1], nvar[:, tc:tc + 1],
                                AF.Ln, bias=c_eps[:, 0:1], scale=-1.0))
        tok_rs = None
        for tc in range(TC):
            tok_rs = p.emit("scalar", lambda s, tc=tc:
                            s.activation(rstd[:, tc:tc + 1],
                                         lncol[:, tc:tc + 1],
                                         AF.Exp, bias=0.0, scale=-0.5),
                            inc="ss" if tc == TC - 1 else None)
        p.need("vector", tok_rs)
        tok_ust = {}
        for tc in range(TC):
            if tc >= 2:
                p.need("vector", tok_ust[tc - 2])
            tok_u = p.emit("vector", lambda v, tc=tc:
                           v.tensor_scalar(
                               u_tm[:, (tc % 2) * D:(tc % 2 + 1) * D],
                               xres[:, tc * D:(tc + 1) * D],
                               mcol[:, tc:tc + 1], rstd[:, tc:tc + 1],
                               ALU.subtract, ALU.mult),
                           inc="sv")
            p.need("sync", tok_u)
            tok_ust[tc] = p.emit("sync", lambda y, tc=tc:
                                 y.dma_start(
                                     sc_u[tc * 128:(tc + 1) * 128, :],
                                     u_tm[:, (tc % 2) * D:(tc % 2 + 1) * D]),
                                 inc=f"u{tc % 2}")
        tok_uT = None
        for dc in range(DC):
            p.need("sync", tok_ust[TC - 1], tok_ust[TC - 2])
            tok_uT = p.emit("sync", lambda y, dc=dc:
                            y.dma_start(uT[:, dc * T:(dc + 1) * T],
                                        sc_u[:, dc * 128:(dc + 1) * 128],
                                        transpose=True),
                            inc="dy")
        p.end()
        return tok_uT

    def gate_and_combine(tok_uT):
        ges = ExitStack()
        lsb = lambda n, s, dt: ges.enter_context(nc.sbuf_tensor(_uid(n), s, dt))
        wg = lsb("wg", [128, DC * NEXP], BF16)
        glog = lsb("glog", [128, 128], F32)
        gsc = lsb("gsc", [128, 128], F32)
        nm1 = lsb("nm1", [128, TC], F32)
        gz = lsb("gz", [128, TC], F32)
        gm1 = lsb("gm1", [128, TC], F32)
        gm2 = lsb("gm2", [128, TC], F32)
        gzr = lsb("gzr", [128, TC], F32)
        gzr5 = lsb("gzr5", [128, TC], F32)
        gmask = lsb("gmask", [128, 128], F32)
        gknock = lsb("gknock", [128, 128], F32)
        p.begin()
        tok_wg = p.emit("gpsimd", lambda g:
                        g.dma_start(wg[:, :].rearrange("p (c e) -> p c e",
                                                       c=DC),
                                    d_Wg.rearrange("(c p) e -> p c e",
                                                   p=128)), inc="dg")
        p.need("tensor", tok_wg, tok_uT, tok_bc)
        tok_gl = None
        for tc in range(TC):
            out = PA[:, tc * 8:(tc + 1) * 8]
            for dc in range(DC):
                p.emit("tensor", lambda t, out=out, dc=dc, tc=tc:
                       t.matmul(out,
                                uT[:, dc * T + tc * 128:
                                   dc * T + (tc + 1) * 128],
                                wg[:, dc * 8:(dc + 1) * 8],
                                start=(tc == 0 and dc == 0), stop=False,
                                skip_group_check=True))
            tok_gl = p.emit("tensor", lambda t, out=out, tc=tc:
                            t.matmul(out, ones_r[0:1, 0:128], r_bg[0:1, :],
                                     start=False, stop=(tc == TC - 1),
                                     skip_group_check=True),
                            inc="st" if tc == TC - 1 else None)
        p.need("vector", tok_gl)
        p.emit("vector", lambda v: v.tensor_copy(glog[:, :], PA[:, 0:128]))
        for tc in range(TC):
            p.emit("vector", lambda v, tc=tc:
                   v.reduce_max(nm1[:, tc:tc + 1],
                                glog[:, tc * 8:(tc + 1) * 8],
                                axis=AX, negate=True))
        tok = None
        for tc in range(TC):
            tok = p.emit("vector", lambda v, tc=tc:
                         v.tensor_scalar_add(gsc[:, tc * 8:(tc + 1) * 8],
                                             glog[:, tc * 8:(tc + 1) * 8],
                                             nm1[:, tc:tc + 1]),
                         inc="sv" if tc == TC - 1 else None)
        p.need("scalar", tok)
        tok = p.emit("scalar", lambda s:
                     s.activation(gsc[:, :], gsc[:, :], AF.Exp), inc="ss")
        p.need("vector", tok)
        for tc in range(TC):
            p.emit("vector", lambda v, tc=tc:
                   v.reduce_sum(gz[:, tc:tc + 1], gsc[:, tc * 8:(tc + 1) * 8],
                                axis=AX))
        for tc in range(TC):
            p.emit("vector", lambda v, tc=tc:
                   v.reduce_max(gm1[:, tc:tc + 1],
                                gsc[:, tc * 8:(tc + 1) * 8], axis=AX))
        for tc in range(TC):
            p.emit("vector", lambda v, tc=tc:
                   v.tensor_single_scalar(gknock[:, tc * 8:(tc + 1) * 8],
                                          gsc[:, tc * 8:(tc + 1) * 8],
                                          gm1[:, tc:tc + 1], ALU.is_lt))
        for tc in range(TC):
            p.emit("vector", lambda v, tc=tc:
                   v.tensor_mul(gknock[:, tc * 8:(tc + 1) * 8],
                                gknock[:, tc * 8:(tc + 1) * 8],
                                gsc[:, tc * 8:(tc + 1) * 8]))
        for tc in range(TC):
            p.emit("vector", lambda v, tc=tc:
                   v.reduce_max(gm2[:, tc:tc + 1],
                                gknock[:, tc * 8:(tc + 1) * 8], axis=AX))
        for tc in range(TC):
            p.emit("vector", lambda v, tc=tc:
                   v.reciprocal(gzr[:, tc:tc + 1], gz[:, tc:tc + 1]))
        for tc in range(TC):
            p.emit("vector", lambda v, tc=tc:
                   v.tensor_single_scalar(gmask[:, tc * 8:(tc + 1) * 8],
                                          gsc[:, tc * 8:(tc + 1) * 8],
                                          gm2[:, tc:tc + 1], ALU.is_ge))
        for tc in range(TC):
            p.emit("vector", lambda v, tc=tc:
                   v.tensor_scalar_mul(gzr5[:, tc:tc + 1], gzr[:, tc:tc + 1],
                                       0.5))
        for tc in range(TC):
            p.emit("vector", lambda v, tc=tc:
                   v.tensor_mul(gmask[:, tc * 8:(tc + 1) * 8],
                                gmask[:, tc * 8:(tc + 1) * 8],
                                gsc[:, tc * 8:(tc + 1) * 8]))
        for tc in range(TC):
            p.emit("vector", lambda v, tc=tc:
                   v.tensor_single_scalar(comb[:, tc * 8:(tc + 1) * 8],
                                          gmask[:, tc * 8:(tc + 1) * 8],
                                          gzr5[:, tc:tc + 1], ALU.mult))
        p.end()
        ges.close()

    def moe_attn(tok_uT, vtok):
        aes = ExitStack()
        lsb = lambda n, s, dt: aes.enter_context(nc.sbuf_tensor(_uid(n), s, dt))
        q2T = lsb("q2T", [128, T], BF16)
        k2T = lsb("k2T", [128, T], BF16)
        ebuf = [lsb(f"ebuf{i}", [128, 512], BF16) for i in range(2)]
        vaug = [lsb(f"vaug{i}", [128, 72], BF16) for i in range(2)]
        attn_ub = lsb("attn_ub", [64, 512], BF16)
        attn_n = [lsb(f"attn_n{i}", [64, 512], BF16) for i in range(2)]
        wqh = [lsb(f"wqh{i}", [128, DC * 128], BF16) for i in range(2)]
        wkh = [lsb(f"wkh{i}", [128, DC * 128], BF16) for i in range(2)]
        wbig = [lsb(f"we{i}", [128, DC * D], BF16) for i in range(2)]
        r_be = [lsb(f"r_be{i}", [1, D], BF16) for i in range(2)]
        mqk = lsb("mqk", [1, 16], F32)
        tsum = lsb("tsum", [1, 1], F32)
        negb_bf = lsb("negb_bf", [1, 1], BF16)
        negb_col = lsb("negb_col", [128, 1], F32)
        rzrow = lsb("rzrow", [1, 512], F32)
        rzb = lsb("rzb", [1, 512], BF16)

        P_sc = [PA[:, 0:512], PA[:, 512:1024]]
        P_moe = [PA[:, 1024:1536], PA[:, 1536:2048]]
        P_av = [PB[:, 0:512], PB[:, 512:1024]]
        P_nb = PB[:, 1024:1025]
        P_bc = PB[:, 1536:2048]

        p.begin()
        for i in range(2):
            p.emit("vector", lambda v, i=i: v.memset(vaug[i][:, 64:72], 0.0))
            p.emit("vector", lambda v, i=i: v.memset(vaug[i][:, 64:65], 2.0))
        tok_we = {}

        def emit_we_load(e):
            p.emit("gpsimd", lambda g, e=e:
                   g.dma_start(r_be[e % 2][:, :], d_be[e:e + 1, :]),
                   inc=f"we{e % 2}")
            tok_we[e] = p.emit("gpsimd", lambda g, e=e:
                               g.dma_start(wbig[e % 2][:, :].rearrange(
                                   "p (c f) -> p c f", c=DC),
                                   d_We[e, :, :].rearrange(
                                       "(c p) f -> p c f", p=128)),
                               inc=f"we{e % 2}")
        emit_we_load(0)
        emit_we_load(1)
        tok_wq = {}
        tok_wk = {}

        def emit_qk_load(hp):
            tok_wq[hp] = p.emit("gpsimd", lambda g, hp=hp:
                                g.dma_start(wqh[hp % 2][:, :].rearrange(
                                    "p (c f) -> p c f", c=DC),
                                    d_Wqkv[:, hp * 128:(hp + 1) * 128]
                                    .rearrange("(c p) f -> p c f", p=128)),
                                inc=f"qk{hp % 2}")
            tok_wk[hp] = p.emit("gpsimd", lambda g, hp=hp:
                                g.dma_start(wkh[hp % 2][:, :].rearrange(
                                    "p (c f) -> p c f", c=DC),
                                    d_Wqkv[:, D + hp * 128:D + (hp + 1) * 128]
                                    .rearrange("(c p) f -> p c f", p=128)),
                                inc=f"qk{hp % 2}")
        emit_qk_load(0)
        emit_qk_load(1)

        moe_slices = [(e, tc, nh) for e in range(NEXP) for tc in range(TC)
                      for nh in range(2)]
        moe_i = [0]
        moe_par = [0]
        tok_moe_ev = {}
        tok_moe_last_mm = {}

        def emit_moe_slice():
            if moe_i[0] >= len(moe_slices):
                return
            e, tc, nh = moe_slices[moe_i[0]]
            moe_i[0] += 1
            par = moe_par[0]
            moe_par[0] ^= 1
            out = P_moe[par]
            p.need("tensor", tok_we[e], tok_uT, tok_moe_ev.get(par))
            for dc in range(DC):
                p.emit("tensor", lambda t, out=out, dc=dc, tc=tc, e=e, nh=nh:
                       t.matmul(out,
                                uT[:, dc * T + tc * 128:dc * T + (tc + 1) * 128],
                                wbig[e % 2][:, dc * D + nh * 512:
                                            dc * D + (nh + 1) * 512],
                                start=(dc == 0), stop=False))
            tok_mm = p.emit("tensor", lambda t, out=out, e=e, nh=nh:
                            t.matmul(out, ones_r[0:1, 0:128],
                                     r_be[e % 2][0:1, nh * 512:(nh + 1) * 512],
                                     start=False, stop=True), inc="st")
            tok_moe_last_mm[e] = tok_mm
            p.need("vector", tok_mm)
            tok_moe_ev[par] = p.emit(
                "vector", lambda v, out=out, tc=tc, e=e, nh=nh:
                v.scalar_tensor_tensor(
                    xres[:, tc * D + nh * 512:tc * D + (nh + 1) * 512],
                    out, comb[:, tc * 8 + e:tc * 8 + e + 1],
                    xres[:, tc * D + nh * 512:tc * D + (nh + 1) * 512],
                    ALU.mult, ALU.add), inc="sv")
            if (tc, nh) == (TC - 1, 1) and e + 2 < NEXP:
                p.need("gpsimd", tok_mm)
                emit_we_load(e + 2)

        tok_psc_free = {0: None, 1: None}
        tok_pav_free = {0: [], 1: []}
        tok_pbc_free = None
        last_av = {}
        tok_qk_free = None
        tok_attn_dma = {}
        step = [0]

        for hp in range(H // 2):
            for qk in range(2):
                p.need("tensor", (tok_wq, tok_wk)[qk][hp], tok_uT, tok_qk_free)
                p.need("scalar", tok_bc)
                w = (wqh, wkh)[qk][hp % 2]
                dst = (q2T, k2T)[qk]
                tok_ev = {}
                for tq in range(4):
                    par = tq % 2
                    p.need("tensor", tok_psc_free[par], tok_ev.get(tq - 2))
                    out = P_sc[par]
                    for dc in range(DC):
                        p.emit("tensor", lambda t, out=out, w=w, dc=dc, tq=tq:
                               t.matmul(out, w[:, dc * 128:(dc + 1) * 128],
                                        uT[:, dc * T + tq * 512:
                                           dc * T + (tq + 1) * 512],
                                        start=(dc == 0), stop=(dc == DC - 1)),
                               inc="st" if dc == DC - 1 else None)
                    tok_m = ("st", p.cnt["st"])
                    p.need("scalar", tok_m)
                    tok_ev[tq] = p.emit(
                        "scalar", lambda s, dst=dst, par=par, tq=tq, qk=qk, hp=hp:
                        s.activation(dst[:, tq * 512:(tq + 1) * 512],
                                     P_sc[par], AF.Identity,
                                     bias=bqkv_cs[:, qk * 8 + hp:qk * 8 + hp + 1],
                                     scale=RS8), inc="ss")
                    tok_psc_free[par] = tok_ev[tq]
                    emit_moe_slice()
                tok_qkprod_last = ("st", p.cnt["st"])
            if hp + 2 < H // 2:
                p.need("gpsimd", tok_qkprod_last)
                emit_qk_load(hp + 2)
            tok_b = {}
            bi = 0
            for qk in range(2):
                src_t = (q2T, k2T)[qk]
                for half in range(4):
                    par = bi % 2
                    p.need("scalar", last_av.get(par))
                    tok_sq = p.emit(
                        "scalar", lambda s, src_t=src_t, half=half, par=par:
                        s.activation(ebuf[par][:, :],
                                     src_t[:, half * 512:(half + 1) * 512],
                                     AF.Square), inc="ss")
                    last_av[par] = tok_sq
                    p.need("tensor", tok_sq, *tok_pav_free[0])
                    p.need("tensor", tok_b.get(bi - 1))
                    p.emit("tensor", lambda t, par=par:
                           t.matmul(P_av[0][0:1, :], ones_c[:, 0:1],
                                    ebuf[par][:, :], start=True, stop=True),
                           inc="st")
                    tok_mmq = ("st", p.cnt["st"])
                    p.need("vector", tok_mmq)
                    tok_b[bi] = p.emit(
                        "vector", lambda v, bi=bi:
                        v.reduce_max(mqk[0:1, bi:bi + 1], P_av[0][0:1, :],
                                     axis=AX), inc="sv")
                    bi += 1
                    emit_moe_slice()
            p.need("scalar", tok_b[7])
            tok_ts = p.emit("scalar", lambda s:
                            s.activation(mqk[0:1, 8:16], mqk[0:1, 0:8],
                                         AF.Identity,
                                         accum_out=tsum[0:1, 0:1]), inc="ss")
            p.need("vector", tok_ts)
            tok_nb = p.emit("vector", lambda v:
                            v.tensor_scalar_mul(negb_bf[0:1, 0:1],
                                                tsum[0:1, 0:1], -0.5),
                            inc="sv")
            p.need("tensor", tok_nb, tok_pbc_free)
            tok_nbc = p.emit("tensor", lambda t:
                             t.matmul(P_nb, ones_r[0:1, 0:128],
                                      negb_bf[0:1, 0:1], start=True, stop=True),
                             inc="st")
            p.need("scalar", tok_nbc)
            tok_nbcol = p.emit("scalar", lambda s:
                               s.copy(negb_col[:, 0:1], P_nb), inc="ss")
            tok_pav_free[0] = [tok_b[7]]
            tok_pbc_free = tok_nbcol

            tok_sc_pair_last = None
            for hh in range(2):
                h = 2 * hp + hh
                rb = hh * 64
                for qq in range(4):
                    avp = qq % 2
                    tok_ex = {}
                    tok_av = {}
                    for kc in range(16):
                        par = kc % 2
                        p.need("tensor", tok_psc_free[par], tok_nbcol)
                        p.emit("tensor", lambda t, rb=rb, kc=kc, qq=qq, par=par:
                               t.matmul(
                                   P_sc[par],
                                   k2T[rb:rb + 64, kc * 128:(kc + 1) * 128],
                                   q2T[rb:rb + 64, qq * 512:(qq + 1) * 512],
                                   start=True, stop=True), inc="st")
                        tok_sc = ("st", p.cnt["st"])
                        tok_sc_pair_last = tok_sc
                        p.need("scalar", tok_sc, tok_nbcol, last_av.get(par))
                        tok_ex[kc] = p.emit(
                            "scalar", lambda s, kc=kc, par=par:
                            s.activation(ebuf[par][:, :], P_sc[par], AF.Exp,
                                         bias=negb_col[:, 0:1], scale=1.0),
                            inc="ss")
                        tok_psc_free[par] = tok_ex[kc]
                        p.need("vector", last_av.get(par))
                        tok_va = p.emit(
                            "vector", lambda v, kc=kc, h=h, par=par:
                            v.tensor_copy(vaug[par][:, 0:64],
                                          vtok[:, kc * D + h * 64:
                                               kc * D + h * 64 + 64]),
                            inc="sv")
                        p.need("tensor", tok_ex[kc], tok_va)
                        if kc == 0:
                            p.need("tensor", *tok_pav_free[avp])
                        p.emit("tensor", lambda t, kc=kc, avp=avp, par=par:
                               t.matmul(P_av[avp][0:65, :],
                                        vaug[par][:, 0:65],
                                        ebuf[par][:, :],
                                        start=(kc == 0), stop=(kc == 15)),
                               inc="st")
                        tok_av[kc] = ("st", p.cnt["st"])
                        last_av[par] = tok_av[kc]
                        if step[0] % 4 == 0:
                            emit_moe_slice()
                        step[0] += 1
                    p.need("vector", tok_av[15])
                    p.emit("vector", lambda v, avp=avp:
                           v.reciprocal(rzrow[0:1, 0:512],
                                        P_av[avp][64:65, :]))
                    tok_rzb = p.emit("vector", lambda v:
                                     v.tensor_copy(rzb[0:1, 0:512],
                                                   rzrow[0:1, 0:512]),
                                     inc="sv")
                    p.need("scalar", tok_av[15])
                    tok_ub = p.emit("scalar", lambda s, avp=avp:
                                    s.copy(attn_ub[0:64, 0:512],
                                           P_av[avp][0:64, :]), inc="ss")
                    p.need("tensor", tok_rzb, tok_pbc_free)
                    p.emit("tensor", lambda t:
                           t.matmul(P_bc[0:64, 0:512], ones_r[0:1, 0:64],
                                    rzb[0:1, 0:512], start=True, stop=True),
                           inc="st")
                    tok_bcm = ("st", p.cnt["st"])
                    p.need("vector", tok_bcm, tok_ub,
                           tok_attn_dma.get((h - 1, qq)))
                    tok_mult = p.emit(
                        "vector", lambda v, avp=avp:
                        v.tensor_mul(attn_n[avp][0:64, 0:512],
                                     attn_ub[0:64, 0:512], P_bc[0:64, 0:512]),
                        inc="sv")
                    tok_pav_free[avp] = [tok_rzb, tok_ub]
                    tok_pbc_free = tok_mult
                    p.need("sync", tok_mult)
                    tok_attn_dma[(h, qq)] = p.emit(
                        "sync", lambda y, h=h, qq=qq, avp=avp:
                        y.dma_start(
                            sc_att[h * 64:(h + 1) * 64,
                                   qq * 512:(qq + 1) * 512],
                            attn_n[avp][0:64, 0:512]), inc=f"ad{avp}")
            tok_qk_free = tok_sc_pair_last
        while moe_i[0] < len(moe_slices):
            emit_moe_slice()
        p.end()

        aes.close()
        tes = ExitStack()
        att_tm = [tes.enter_context(nc.sbuf_tensor(_uid(f"att_tm{i}"),
                                                   [128, D], BF16))
                  for i in range(2)]
        p.begin()
        tok_add = {}
        for tc in range(TC):
            p.need("sync", *[tok_attn_dma[(H - 1, q)] for q in range(4)])
            if tc >= 2:
                p.need("sync", tok_add[tc - 2])
            tok_tr = p.emit("sync", lambda y, tc=tc:
                            y.dma_start(att_tm[tc % 2][:, :],
                                        sc_att[:, tc * 128:(tc + 1) * 128],
                                        transpose=True), inc=f"at{tc % 2}")
            p.need("vector", tok_tr)
            tok_add[tc] = p.emit("vector", lambda v, tc=tc:
                                 v.tensor_add(xres[:, tc * D:(tc + 1) * D],
                                              xres[:, tc * D:(tc + 1) * D],
                                              att_tm[tc % 2][:, :]),
                                 inc="sv")
        p.end()
        tes.close()

    def ffn():
        tok_uT = layernorm_to_uT()
        gate_and_combine(tok_uT)
        ves = ExitStack()
        vtok = ves.enter_context(nc.sbuf_tensor(_uid("vtok"), [128, TC * D], BF16))
        wes = ExitStack()
        wv = wes.enter_context(nc.sbuf_tensor(_uid("wv"), [128, DC * D], BF16))
        p.begin()
        tok_wv = p.emit("gpsimd", lambda g:
                        g.dma_start(wv[:, :].rearrange("p (c f) -> p c f",
                                                       c=DC),
                                    d_Wqkv[:, 2 * D:3 * D].rearrange(
                                        "(c p) f -> p c f", p=128)),
                        inc="dg")
        tok_mm = {}
        tok_ev = {}
        for tc in range(TC):
            p.need("tensor", tok_wv, tok_uT, tok_bc)
            pt = (PA, PB)[tc % 2]
            if tc >= 4:
                p.need("tensor", tok_ev[tc - 4])
            for nh in range(2):
                out = pt[:, nh * 512:(nh + 1) * 512]
                for dc in range(DC):
                    p.emit("tensor", lambda t, out=out, dc=dc, tc=tc, nh=nh:
                           t.matmul(out,
                                    uT[:, dc * T + tc * 128:
                                       dc * T + (tc + 1) * 128],
                                    wv[:, dc * D + nh * 512:
                                       dc * D + (nh + 1) * 512],
                                    start=(dc == 0), stop=False))
                tok_mm[tc] = p.emit("tensor", lambda t, out=out, nh=nh:
                                    t.matmul(out, ones_r[0:1, 0:128],
                                             r_bqkv[0:1, nh * 512:
                                                    (nh + 1) * 512],
                                             start=False, stop=True),
                                    inc="st" if nh == 1 else None)
            p.need("scalar", tok_mm[tc])
            tok_ev[tc] = p.emit("scalar", lambda s, pt=pt, tc=tc:
                                s.copy(vtok[:, tc * D:(tc + 1) * D],
                                       pt[:, 0:1024]), inc="ss")
        p.end()
        wes.close()
        moe_attn(tok_uT, vtok)
        ves.close()

    def mingru(_dbg_done=[]):
        tok_uT = layernorm_to_uT()
        mes = ExitStack()
        lsb = lambda n, s, dt: mes.enter_context(nc.sbuf_tensor(_uid(n), s, dt))
        wkt = [lsb(f"wkt{i}", [128, 3 * DC * 128], BF16) for i in range(2)]
        fbuf = lsb("fbuf", [128, T], F32)
        vbuf = lsb("vbuf", [128, T], F32)
        kh05 = lsb("kh05", [128, T], F32)
        shb = lsb("shb", [128, T], F32)
        gbuf = lsb("gbuf", [128, T], F32)
        mskb = lsb("mskb", [128, T], U8)
        h_bf = lsb("h_bf", [128, T], BF16)

        p.begin()
        tok_wl = {}
        tok_mm = {}
        tok_fe = {}
        tok_ie = {}
        tok_he = {}
        tok_scan = {}
        tok_hdma = {}

        def emit_wkt_load(ec):
            p.need("gpsimd", tok_wt_done)
            for ci in range(3):
                tok_wl[ec] = p.emit(
                    "gpsimd", lambda g, ec=ec, ci=ci:
                    g.dma_start(
                        wkt[ec % 2][:, ci * DC * 128:(ci + 1) * DC * 128]
                        .rearrange("p (c e) -> p c e", c=DC),
                        sc_WT.rearrange("(c p) j -> p c j", p=128)[
                            :, :, ci * EMG + ec * 128:
                            ci * EMG + (ec + 1) * 128]),
                    inc=f"wk{ec % 2}")
        emit_wkt_load(0)
        emit_wkt_load(1)
        for ec in range(EC):
            p.need("tensor", tok_wl[ec], tok_uT)
            p.need("scalar", tok_b5v, tok_bc)
            for ci in range(3):
                pt = (PA, PB, PA)[ci]
                if ci == 0 and ec >= 1:
                    p.need("tensor", tok_he[ec - 1])
                if ci == 1 and ec >= 1:
                    p.need("tensor", tok_ie[ec - 1])
                if ci == 2:
                    p.need("tensor", tok_fe[ec])
                for tq in range(4):
                    out = pt[:, tq * 512:(tq + 1) * 512]
                    for dc in range(DC):
                        tok_mm[(ec, ci)] = p.emit(
                            "tensor",
                            lambda t, out=out, ec=ec, ci=ci, dc=dc, tq=tq:
                            t.matmul(out,
                                     wkt[ec % 2][:,
                                                 ci * DC * 128 + dc * 128:
                                                 ci * DC * 128 +
                                                 (dc + 1) * 128],
                                     uT[:, dc * T + tq * 512:
                                        dc * T + (tq + 1) * 512],
                                     start=(dc == 0), stop=(dc == DC - 1)),
                            inc="st" if (tq == 3 and dc == DC - 1) else None)
                p.need("scalar", tok_mm[(ec, ci)])
                if ec >= 1:
                    p.need("scalar", tok_scan[ec - 1])
                if ci == 0:
                    tok_fe[ec] = p.emit("scalar", lambda s, ec=ec:
                                        s.activation(fbuf[:, :], PA[:, :],
                                                     AF.Sigmoid,
                                                     bias=batt_cs[:, ec:ec + 1]),
                                        inc="ss")
                elif ci == 1:
                    tok_ie[ec] = p.emit("scalar", lambda s, ec=ec:
                                        s.activation(vbuf[:, :], PB[:, :],
                                                     AF.Sigmoid,
                                                     bias=batt_cs[:, 16 + ec:17 + ec]),
                                        inc="ss")
                else:
                    p.emit("scalar", lambda s, ec=ec:
                           s.activation(kh05[:, :], PA[:, :], AF.Identity,
                                        bias=bh05_cs[:, ec:ec + 1]))
                    tok_he[ec] = p.emit("scalar", lambda s, ec=ec:
                                        s.activation(shb[:, :], PA[:, :],
                                                     AF.Sigmoid,
                                                     bias=batt_cs[:, 32 + ec:33 + ec],
                                                     scale=1.0), inc="ss")
            p.need("vector", tok_he[ec], tok_ie[ec], tok_fe[ec])
            p.emit("vector", lambda v:
                   v.tensor_single_scalar(mskb[:, :], kh05[:, :], 0.5,
                                          ALU.is_ge))
            p.emit("vector", lambda v:
                   v.select(gbuf[:, :], mskb[:, :], kh05[:, :], shb[:, :]))
            p.emit("vector", lambda v:
                   v.tensor_mul(vbuf[:, :], vbuf[:, :], gbuf[:, :]))
            if ec >= 1:
                p.need("vector", tok_hdma[ec - 1])
            tok_scan[ec] = p.emit("vector", lambda v:
                                  v.tensor_tensor_scan(h_bf[:, :], fbuf[:, :],
                                                       vbuf[:, :], 0.5,
                                                       ALU.mult, ALU.add),
                                  inc="sv")
            p.need("sync", tok_scan[ec])
            tok_hdma[ec] = p.emit("sync", lambda y, ec=ec:
                                  y.dma_start(sc_h[ec, :, :], h_bf[:, :]),
                                  inc="dy")

            if ec + 2 < EC:
                p.need("gpsimd", tok_mm[(ec, 2)])
                emit_wkt_load(ec + 2)
        if DEBUG_TAPS and not _dbg_done:
            _dbg_done.append(1)
            p.need("sync", tok_hdma[EC - 1])
            p.emit("sync", lambda y: y.dma_start(
                dbg_h.rearrange("(e p) t -> e p t", p=128), sc_h[:, :, :]),
                inc="dy")
            p.emit("sync", lambda y: y.dma_start(dbg_w, sc_WT[:, :]),
                   inc="dy")
        p.end()
        mes.close()

        oes = ExitStack()
        wdt_s = oes.enter_context(nc.sbuf_tensor(_uid("wdt_s"), [128, EC * 512],
                                                 BF16))
        h_sl = [oes.enter_context(nc.sbuf_tensor(_uid(f"h_sl{i}"), [128, EC * 128],
                                                 BF16)) for i in range(2)]
        p.begin()
        tok_pmm = None
        for half in range(2):
            if half == 1:
                p.need("gpsimd", tok_pmm)
            p.need("gpsimd", tok_wdt_done)
            tok_wd = p.emit(
                "gpsimd", lambda g, half=half:
                g.dma_start(
                    wdt_s[:, :].rearrange("p (e d) -> p e d", e=EC),
                    sc_WdT.rearrange("(e p) d -> p e d", p=128)[
                        :, :, half * 512:(half + 1) * 512]), inc="wd0")
            tok_hl = {}
            if half == 0:
                tok_acc = {}
            for tc in range(TC):
                p.need("sync", tok_hdma[EC - 1])
                if tc >= 2:
                    p.need("sync", tok_acc[half * TC + tc - 2])
                elif half == 1:
                    p.need("sync", tok_acc[TC - 2 + tc])
                tok_hl[tc] = p.emit(
                    "sync", lambda y, tc=tc:
                    y.dma_start(h_sl[tc % 2][:, :].rearrange(
                        "p (e t) -> p e t", e=EC),
                        sc_h[:, :, tc * 128:(tc + 1) * 128].rearrange(
                            "e p t -> p e t")), inc=f"hs{tc % 2}")
                p.need("tensor", tok_wd, tok_hl[tc])
                if tc >= 2:
                    p.need("tensor", tok_acc[half * TC + tc - 2])
                pt = (PA, PB)[tc % 2]
                for ec in range(EC):
                    tok_pmm = p.emit(
                        "tensor", lambda t, pt=pt, ec=ec, tc=tc:
                        t.matmul(pt[:, 0:512],
                                 h_sl[tc % 2][:, ec * 128:(ec + 1) * 128],
                                 wdt_s[:, ec * 512:(ec + 1) * 512],
                                 start=(ec == 0), stop=(ec == EC - 1)),
                        inc="st" if ec == EC - 1 else None)
                p.need("vector", tok_pmm)
                tok_acc[half * TC + tc] = p.emit(
                    "vector", lambda v, pt=pt, tc=tc, half=half:
                    v.tensor_add(
                        xres[:, tc * D + half * 512:
                             tc * D + (half + 1) * 512],
                        xres[:, tc * D + half * 512:
                             tc * D + (half + 1) * 512],
                        pt[:, 0:512]), inc="sv")
        p.end()
        oes.close()

    for seq in range(SPC):
        p.begin()
        tok_x = p.emit("sync", lambda y, seq=seq:
                       y.dma_start(xres[:, :].rearrange("p (tc d) -> p tc d", tc=TC),
                                   d_x[seq * T:(seq + 1) * T, :].rearrange(
                                       "(tc p) d -> p tc d", p=128)),
                       inc="dy")
        p.need("sync", tok_x)
        p.end()

        ffn()
        if STAGE >= 2:
            mingru()
        if STAGE >= 3:
            ffn()

        p.begin()
        tok_y = p.emit("sync", lambda y, seq=seq:
                       y.dma_start(d_y[seq * T:(seq + 1) * T, :].rearrange(
                           "(tc p) d -> p tc d", p=128),
                           xres[:, :].rearrange("p (tc d) -> p tc d", tc=TC)),
                       inc="dy")
        p.need("sync", tok_y)
        p.end()

    es.close()
    nc.reset()
    return nc


_NC_CACHE = None


def kernel(L0_attn_x, L0_attn_W, L0_attn_b, L0_attn_Wd, L0_ffn_W_qkv,
           L0_ffn_b_qkv, L0_ffn_W_gate, L0_ffn_b_gate, L0_ffn_W_experts,
           L0_ffn_b_experts):
    global _NC_CACHE
    x = np.ascontiguousarray(np.asarray(L0_attn_x, np.float32))
    B = x.shape[0]
    xf = x.reshape(B * T, D)
    if _NC_CACHE is None:
        _NC_CACHE = build_program()
    nc = _NC_CACHE
    common = {
        "W": np.ascontiguousarray(np.asarray(L0_attn_W, np.float32)),
        "b": np.ascontiguousarray(np.asarray(L0_attn_b, np.float32)).reshape(1, E3),
        "Wd": np.ascontiguousarray(np.asarray(L0_attn_Wd, np.float32)),
        "Wqkv": np.ascontiguousarray(np.asarray(L0_ffn_W_qkv, np.float32)),
        "bqkv": np.ascontiguousarray(np.asarray(L0_ffn_b_qkv, np.float32)).reshape(1, 3 * D),
        "Wg": np.ascontiguousarray(np.asarray(L0_ffn_W_gate, np.float32)),
        "bg": np.ascontiguousarray(np.asarray(L0_ffn_b_gate, np.float32)).reshape(1, NEXP),
        "We": np.ascontiguousarray(np.asarray(L0_ffn_W_experts, np.float32)),
        "be": np.ascontiguousarray(np.asarray(L0_ffn_b_experts, np.float32)),
    }
    in_maps = []
    for c in range(NCORES):
        m = dict(common)
        m["x"] = np.ascontiguousarray(xf[c * SPC * T:(c + 1) * SPC * T, :])
        in_maps.append(m)
    res = run_bass_kernel_spmd(nc, in_maps, core_ids=list(range(NCORES)))
    y = np.concatenate([res.results[c]["y"] for c in range(NCORES)], axis=0)
    return y.reshape(B, T, D)


# version 05
# speedup vs baseline: 1.4032x; 1.4032x over previous
import numpy as np
import concourse.bass as bass
import concourse.mybir as mybir
from concourse.bass_utils import run_bass_kernel_spmd
from contextlib import ExitStack

F32 = mybir.dt.float32
BF16 = mybir.dt.bfloat16
U8 = mybir.dt.uint8
AF = mybir.ActivationFunctionType
ALU = mybir.AluOpType
AX = mybir.AxisListType.X

NCORES = 8
SPC = 2
T = 2048
D = 1024
TC = T // 128
DC = D // 128
H = 16
NEXP = 8
E3 = 6144
EMG = 2048
EC = EMG // 128
RS8 = 0.35355339059327373
STAGE = 3
MOE_ON = True
ATT_ON = True
DEBUG_TAPS = False
DEBUG_EC = 0

ENGS = ("tensor", "vector", "scalar", "gpsimd", "sync")


class Prog:

    def __init__(self, nc, es):
        self.nc = nc
        self.sems = {}
        for name in ("st", "sv", "ss", "dg", "dy", "we0", "we1", "qk0",
                     "qk1", "wk0", "wk1", "hs0", "hs1", "at0", "at1",
                     "u0", "u1", "ad0", "ad1", "wd0", "bc"):
            self.sems[name] = es.enter_context(nc.semaphore(name))
        self.cnt = {k: 0 for k in self.sems}
        self.dma_step = {k: 16 for k in self.sems
                         if k not in ("st", "sv", "ss")}
        self.wm = {e: {k: 0 for k in self.sems} for e in ENGS}
        self.cur = None
        self.n_inst = 0

    def begin(self):
        assert self.cur is None
        self.cur = {e: [] for e in ENGS}

    def end(self):
        q = self.cur
        self.cur = None
        with self.nc.Block() as blk:
            for e in ENGS:
                if q[e]:
                    def runner(eng, lst=q[e]):
                        for th in lst:
                            th(eng)
                    getattr(blk, e)(runner)
        for e in ENGS:
            for k in ("st", "sv", "ss"):
                self.wm[e][k] = max(self.wm[e][k], self.cnt[k])

    def need(self, eng, *tokens):
        for tok in tokens:
            if tok is None:
                continue
            k, c = tok
            if self.wm[eng][k] >= c:
                continue
            self.wm[eng][k] = c
            sem = self.sems[k]
            self.cur[eng].append(lambda E, sem=sem, c=c: E.wait_ge(sem, c))
            self.n_inst += 1

    def emit(self, eng, fn, inc=None):
        self.n_inst += 1
        if inc is not None:
            step = self.dma_step.get(inc, 1)
            self.cnt[inc] += step
            tok = (inc, self.cnt[inc])
            sem = self.sems[inc]

            def th(E, fn=fn, sem=sem, step=step):
                fn(E).then_inc(sem, step)
            self.cur[eng].append(th)
            return tok
        self.cur[eng].append(lambda E, fn=fn: fn(E))
        return None


def build_program():
    nc = bass.Bass("TRN2", target_bir_lowering=False)

    d_x = nc.dram_tensor("x", [SPC * T, D], F32, kind="ExternalInput").ap()
    d_W = nc.dram_tensor("W", [E3, D], F32, kind="ExternalInput").ap()
    d_b = nc.dram_tensor("b", [1, E3], F32, kind="ExternalInput").ap()
    d_Wd = nc.dram_tensor("Wd", [D, EMG], F32, kind="ExternalInput").ap()
    d_Wqkv = nc.dram_tensor("Wqkv", [D, 3 * D], F32, kind="ExternalInput").ap()
    d_bqkv = nc.dram_tensor("bqkv", [1, 3 * D], F32, kind="ExternalInput").ap()
    d_Wg = nc.dram_tensor("Wg", [D, NEXP], F32, kind="ExternalInput").ap()
    d_bg = nc.dram_tensor("bg", [1, NEXP], F32, kind="ExternalInput").ap()
    d_We = nc.dram_tensor("We", [NEXP, D, D], F32, kind="ExternalInput").ap()
    d_be = nc.dram_tensor("be", [NEXP, D], F32, kind="ExternalInput").ap()
    d_y = nc.dram_tensor("y", [SPC * T, D], F32, kind="ExternalOutput").ap()

    if DEBUG_TAPS:
        dbg_h = nc.dram_tensor("dbg_h", [EC * 128, T], BF16, kind="ExternalOutput").ap()
        dbg_w = nc.dram_tensor("dbg_w", [D, E3], BF16, kind="ExternalOutput").ap()
    sc_u = nc.dram_tensor("sc_u", [T, D], BF16, kind="Internal").ap()
    sc_att = nc.dram_tensor("sc_att", [D, T], BF16, kind="Internal").ap()
    sc_h = nc.dram_tensor("sc_h", [EC, 128, T], BF16, kind="Internal").ap()
    sc_Wbf = nc.dram_tensor("sc_Wbf", [E3, D], BF16, kind="Internal").ap()
    sc_WT = nc.dram_tensor("sc_WT", [D, E3], BF16, kind="Internal").ap()
    sc_Wdbf = nc.dram_tensor("sc_Wdbf", [D, EMG], BF16, kind="Internal").ap()
    sc_WdT = nc.dram_tensor("sc_WdT", [EMG, D], BF16, kind="Internal").ap()

    es = ExitStack()
    sbt = lambda n, s, dt: es.enter_context(nc.sbuf_tensor(n, s, dt))
    pst = lambda n, s: es.enter_context(nc.psum_tensor(n, s, F32))

    xres = sbt("xres", [128, TC * D], F32)
    uT = sbt("uT", [128, DC * T], BF16)
    u_tm = sbt("u_tm", [128, 2 * D], BF16)
    sx = sbt("sx", [128, TC], F32)
    sxx = sbt("sxx", [128, TC], F32)
    mcol = sbt("mcol", [128, TC], F32)
    ex2c = sbt("ex2c", [128, TC], F32)
    nvar = sbt("nvar", [128, TC], F32)
    lncol = sbt("lncol", [128, TC], F32)
    rstd = sbt("rstd", [128, TC], F32)
    r_bqkv = sbt("r_bqkv", [1, D], BF16)
    r_bg = sbt("r_bg", [1, NEXP], BF16)
    ones_r = sbt("ones_r", [1, 2048], BF16)
    ones_c = sbt("ones_c", [128, 1], BF16)
    c_eps = sbt("c_eps", [128, 1], F32)
    c_nh = sbt("c_nh", [128, 1], F32)
    bqkv_cs = sbt("bqkv_cs", [128, 16], F32)
    batt_cs = sbt("batt_cs", [128, 48], F32)
    bh05_cs = sbt("bh05_cs", [128, 16], F32)
    comb = sbt("comb", [128, 128], F32)

    PA = pst("PA", [128, 2048])
    PB = pst("PB", [128, 2048])

    p = Prog(nc, es)
    _uidc = [0]

    def _uid(n):
        _uidc[0] += 1
        return f"{n}_{_uidc[0]}"

    p0es = ExitStack()
    wstrip = p0es.enter_context(nc.sbuf_tensor("wstrip", [128, 8 * D], BF16))
    wtstrip = p0es.enter_context(nc.sbuf_tensor("wtstrip", [128, E3], BF16))

    p.begin()
    p.emit("gpsimd", lambda g: g.dma_start(r_bqkv[:, :],
                                           d_bqkv[0:1, 2 * D:3 * D]), inc="bc")
    p.emit("gpsimd", lambda g: g.dma_start(r_bg[:, :], d_bg), inc="bc")
    p.emit("vector", lambda v: v.memset(ones_r[0:1, :], 1.0))
    p.emit("vector", lambda v: v.memset(ones_c[:, 0:1], 1.0))
    p.emit("vector", lambda v: v.memset(c_eps[:, 0:1], 1e-5))
    p.emit("vector", lambda v: v.memset(c_nh[:, 0:1], -0.5))
    p.emit("gpsimd", lambda g: g.dma_start(
        bqkv_cs[:, :],
        d_bqkv[0:1, 0:2048].rearrange("o (c p) -> p (o c)", p=128),
        allow_slow_non_contiguous=True), inc="bc")
    tok_bc = p.emit("gpsimd", lambda g: g.dma_start(
        batt_cs[:, :],
        d_b[0:1, :].rearrange("o (c p) -> p (o c)", p=128),
        allow_slow_non_contiguous=True), inc="bc")
    p.need("vector", tok_bc)
    p.emit("vector", lambda v: v.tensor_scalar_mul(bqkv_cs[:, :],
                                                   bqkv_cs[:, :], RS8))
    tok_b5v = p.emit("vector", lambda v: v.tensor_scalar_add(
        bh05_cs[:, :], batt_cs[:, 32:48], 0.5), inc="sv")
    p.end()

    tok_wt_done = None
    tok_wdt_done = None
    if STAGE >= 2:
        p.begin()
        tok_store = None
        for i in range(6):
            if tok_store is not None:
                p.need("gpsimd", tok_store)
            tok_ld = p.emit("gpsimd", lambda g, i=i: g.dma_start(
                wstrip[:, :].rearrange("p (c d) -> p c d", c=8),
                d_W[i * 1024:(i + 1) * 1024, :].rearrange(
                    "(c p) d -> p c d", p=128)), inc="dg")
            p.need("sync", tok_ld)
            tok_store = p.emit("sync", lambda y, i=i: y.dma_start(
                sc_Wbf[i * 1024:(i + 1) * 1024, :].rearrange(
                    "(c p) d -> p c d", p=128),
                wstrip[:, :].rearrange("p (c d) -> p c d", c=8)), inc="dy")
        for dc in range(DC):
            p.need("sync", tok_store)
            tok_tr = p.emit("sync", lambda y, dc=dc: y.dma_start(
                wtstrip[:, :], sc_Wbf[:, dc * 128:(dc + 1) * 128],
                transpose=True), inc="dy")
            p.need("sync", tok_tr)
            tok_wt_done = p.emit("sync", lambda y, dc=dc: y.dma_start(
                sc_WT[dc * 128:(dc + 1) * 128, :],
                wtstrip[:, :]), inc="dy")
        for i in range(2):
            p.need("gpsimd", tok_store)
            tok_ld = p.emit("gpsimd", lambda g, i=i: g.dma_start(
                wstrip[:, 0:4 * EMG].rearrange("p (c e) -> p c e", c=4),
                d_Wd[i * 512:(i + 1) * 512, :].rearrange(
                    "(c p) e -> p c e", p=128)), inc="dg")
            p.need("sync", tok_ld)
            tok_store = p.emit("sync", lambda y, i=i: y.dma_start(
                sc_Wdbf[i * 512:(i + 1) * 512, :].rearrange(
                    "(c p) e -> p c e", p=128),
                wstrip[:, 0:4 * EMG].rearrange("p (c e) -> p c e", c=4)),
                inc="dy")
        for ec in range(EC):
            p.need("sync", tok_store, tok_wt_done)
            tok_tr = p.emit("sync", lambda y, ec=ec: y.dma_start(
                wtstrip[:, 0:D], sc_Wdbf[:, ec * 128:(ec + 1) * 128],
                transpose=True), inc="dy")
            p.need("sync", tok_tr)
            tok_wdt_done = p.emit("sync", lambda y, ec=ec: y.dma_start(
                sc_WdT[ec * 128:(ec + 1) * 128, :],
                wtstrip[:, 0:D]), inc="dy")
        p.end()
    p0es.close()

    def layernorm_to_uT():
        p.begin()
        for tc in range(TC):
            xt = xres[:, tc * D:(tc + 1) * D]
            p.emit("vector", lambda v, xt=xt, tc=tc:
                   v.reduce_sum(sx[:, tc:tc + 1], xt, axis=AX))
            p.emit("vector", lambda v, xt=xt, tc=tc:
                   v.scalar_tensor_tensor(uT[:, 0:D], xt, 1.0, xt,
                                          ALU.bypass, ALU.mult,
                                          accum_out=sxx[:, tc:tc + 1]))
        for tc in range(TC):
            p.emit("vector", lambda v, tc=tc:
                   v.tensor_scalar_mul(mcol[:, tc:tc + 1], sx[:, tc:tc + 1],
                                       1.0 / D))
        for tc in range(TC):
            p.emit("vector", lambda v, tc=tc:
                   v.tensor_scalar_mul(ex2c[:, tc:tc + 1], sxx[:, tc:tc + 1],
                                       1.0 / D))
        tok_nv = None
        for tc in range(TC):
            tok_nv = p.emit("vector", lambda v, tc=tc:
                            v.scalar_tensor_tensor(
                                nvar[:, tc:tc + 1], mcol[:, tc:tc + 1],
                                mcol[:, tc:tc + 1], ex2c[:, tc:tc + 1],
                                ALU.mult, ALU.subtract),
                            inc="sv" if tc == TC - 1 else None)
        p.need("scalar", tok_nv)
        for tc in range(TC):
            p.emit("scalar", lambda s, tc=tc:
                   s.activation(lncol[:, tc:tc + 1], nvar[:, tc:tc + 1],
                                AF.Ln, bias=c_eps[:, 0:1], scale=-1.0))
        tok_rs = None
        for tc in range(TC):
            tok_rs = p.emit("scalar", lambda s, tc=tc:
                            s.activation(rstd[:, tc:tc + 1],
                                         lncol[:, tc:tc + 1],
                                         AF.Exp, bias=0.0, scale=-0.5),
                            inc="ss" if tc == TC - 1 else None)
        p.need("vector", tok_rs)
        tok_ust = {}
        for tc in range(TC):
            if tc >= 2:
                p.need("vector", tok_ust[tc - 2])
            tok_u = p.emit("vector", lambda v, tc=tc:
                           v.tensor_scalar(
                               u_tm[:, (tc % 2) * D:(tc % 2 + 1) * D],
                               xres[:, tc * D:(tc + 1) * D],
                               mcol[:, tc:tc + 1], rstd[:, tc:tc + 1],
                               ALU.subtract, ALU.mult),
                           inc="sv")
            p.need("sync", tok_u)
            tok_ust[tc] = p.emit("sync", lambda y, tc=tc:
                                 y.dma_start(
                                     sc_u[tc * 128:(tc + 1) * 128, :],
                                     u_tm[:, (tc % 2) * D:(tc % 2 + 1) * D]),
                                 inc=f"u{tc % 2}")
        tok_uT = None
        for dc in range(DC):
            p.need("sync", tok_ust[TC - 1], tok_ust[TC - 2])
            tok_uT = p.emit("sync", lambda y, dc=dc:
                            y.dma_start(uT[:, dc * T:(dc + 1) * T],
                                        sc_u[:, dc * 128:(dc + 1) * 128],
                                        transpose=True),
                            inc="dy")
        p.end()
        return tok_uT

    def gate_and_combine(tok_uT):
        ges = ExitStack()
        lsb = lambda n, s, dt: ges.enter_context(nc.sbuf_tensor(_uid(n), s, dt))
        wg = lsb("wg", [128, DC * NEXP], BF16)
        glog = lsb("glog", [128, 128], F32)
        gsc = lsb("gsc", [128, 128], F32)
        nm1 = lsb("nm1", [128, TC], F32)
        gz = lsb("gz", [128, TC], F32)
        gm1 = lsb("gm1", [128, TC], F32)
        gm2 = lsb("gm2", [128, TC], F32)
        gzr = lsb("gzr", [128, TC], F32)
        gzr5 = lsb("gzr5", [128, TC], F32)
        gmask = lsb("gmask", [128, 128], F32)
        gknock = lsb("gknock", [128, 128], F32)
        p.begin()
        tok_wg = p.emit("gpsimd", lambda g:
                        g.dma_start(wg[:, :].rearrange("p (c e) -> p c e",
                                                       c=DC),
                                    d_Wg.rearrange("(c p) e -> p c e",
                                                   p=128)), inc="dg")
        p.need("tensor", tok_wg, tok_uT, tok_bc)
        tok_gl = None
        for tc in range(TC):
            out = PA[:, tc * 8:(tc + 1) * 8]
            for dc in range(DC):
                p.emit("tensor", lambda t, out=out, dc=dc, tc=tc:
                       t.matmul(out,
                                uT[:, dc * T + tc * 128:
                                   dc * T + (tc + 1) * 128],
                                wg[:, dc * 8:(dc + 1) * 8],
                                start=(tc == 0 and dc == 0), stop=False,
                                skip_group_check=True))
            tok_gl = p.emit("tensor", lambda t, out=out, tc=tc:
                            t.matmul(out, ones_r[0:1, 0:128], r_bg[0:1, :],
                                     start=False, stop=(tc == TC - 1),
                                     skip_group_check=True),
                            inc="st" if tc == TC - 1 else None)
        p.need("vector", tok_gl)
        p.emit("vector", lambda v: v.tensor_copy(glog[:, :], PA[:, 0:128]))
        for tc in range(TC):
            p.emit("vector", lambda v, tc=tc:
                   v.reduce_max(nm1[:, tc:tc + 1],
                                glog[:, tc * 8:(tc + 1) * 8],
                                axis=AX, negate=True))
        tok = None
        for tc in range(TC):
            tok = p.emit("vector", lambda v, tc=tc:
                         v.tensor_scalar_add(gsc[:, tc * 8:(tc + 1) * 8],
                                             glog[:, tc * 8:(tc + 1) * 8],
                                             nm1[:, tc:tc + 1]),
                         inc="sv" if tc == TC - 1 else None)
        p.need("scalar", tok)
        tok = p.emit("scalar", lambda s:
                     s.activation(gsc[:, :], gsc[:, :], AF.Exp), inc="ss")
        p.need("vector", tok)
        for tc in range(TC):
            p.emit("vector", lambda v, tc=tc:
                   v.reduce_sum(gz[:, tc:tc + 1], gsc[:, tc * 8:(tc + 1) * 8],
                                axis=AX))
        for tc in range(TC):
            p.emit("vector", lambda v, tc=tc:
                   v.reduce_max(gm1[:, tc:tc + 1],
                                gsc[:, tc * 8:(tc + 1) * 8], axis=AX))
        for tc in range(TC):
            p.emit("vector", lambda v, tc=tc:
                   v.tensor_single_scalar(gknock[:, tc * 8:(tc + 1) * 8],
                                          gsc[:, tc * 8:(tc + 1) * 8],
                                          gm1[:, tc:tc + 1], ALU.is_lt))
        for tc in range(TC):
            p.emit("vector", lambda v, tc=tc:
                   v.tensor_mul(gknock[:, tc * 8:(tc + 1) * 8],
                                gknock[:, tc * 8:(tc + 1) * 8],
                                gsc[:, tc * 8:(tc + 1) * 8]))
        for tc in range(TC):
            p.emit("vector", lambda v, tc=tc:
                   v.reduce_max(gm2[:, tc:tc + 1],
                                gknock[:, tc * 8:(tc + 1) * 8], axis=AX))
        for tc in range(TC):
            p.emit("vector", lambda v, tc=tc:
                   v.reciprocal(gzr[:, tc:tc + 1], gz[:, tc:tc + 1]))
        for tc in range(TC):
            p.emit("vector", lambda v, tc=tc:
                   v.tensor_single_scalar(gmask[:, tc * 8:(tc + 1) * 8],
                                          gsc[:, tc * 8:(tc + 1) * 8],
                                          gm2[:, tc:tc + 1], ALU.is_ge))
        for tc in range(TC):
            p.emit("vector", lambda v, tc=tc:
                   v.tensor_scalar_mul(gzr5[:, tc:tc + 1], gzr[:, tc:tc + 1],
                                       0.5))
        for tc in range(TC):
            p.emit("vector", lambda v, tc=tc:
                   v.tensor_mul(gmask[:, tc * 8:(tc + 1) * 8],
                                gmask[:, tc * 8:(tc + 1) * 8],
                                gsc[:, tc * 8:(tc + 1) * 8]))
        for tc in range(TC):
            p.emit("vector", lambda v, tc=tc:
                   v.tensor_single_scalar(comb[:, tc * 8:(tc + 1) * 8],
                                          gmask[:, tc * 8:(tc + 1) * 8],
                                          gzr5[:, tc:tc + 1], ALU.mult))
        p.end()
        ges.close()

    def moe_attn(tok_uT, vtok):
        aes = ExitStack()
        lsb = lambda n, s, dt: aes.enter_context(nc.sbuf_tensor(_uid(n), s, dt))
        q2T = lsb("q2T", [128, T], BF16)
        k2T = lsb("k2T", [128, T], BF16)
        ebuf = [lsb(f"ebuf{i}", [128, 512], BF16) for i in range(2)]
        vaug = [lsb(f"vaug{i}", [128, 72], BF16) for i in range(2)]
        attn_ub = lsb("attn_ub", [64, 512], BF16)
        attn_n = [lsb(f"attn_n{i}", [64, 512], BF16) for i in range(2)]
        wqh = [lsb(f"wqh{i}", [128, DC * 128], BF16) for i in range(2)]
        wkh = [lsb(f"wkh{i}", [128, DC * 128], BF16) for i in range(2)]
        wbig = [lsb(f"we{i}", [128, DC * D], BF16) for i in range(2)]
        r_be = [lsb(f"r_be{i}", [1, D], BF16) for i in range(2)]
        mqk = lsb("mqk", [1, 16], F32)
        tsum = lsb("tsum", [1, 1], F32)
        negb_bf = lsb("negb_bf", [1, 1], BF16)
        negb_col = lsb("negb_col", [128, 1], F32)
        rzrow = lsb("rzrow", [1, 512], F32)
        rzb = lsb("rzb", [1, 512], BF16)

        P_sc = [PA[:, 0:512], PA[:, 512:1024]]
        P_moe = [PA[:, 1024:1536], PA[:, 1536:2048]]
        P_av = [PB[:, 0:512], PB[:, 512:1024]]
        P_nb = PB[:, 1024:1025]
        P_bc = PB[:, 1536:2048]

        p.begin()
        for i in range(2):
            p.emit("vector", lambda v, i=i: v.memset(vaug[i][:, 64:72], 0.0))
            p.emit("vector", lambda v, i=i: v.memset(vaug[i][:, 64:65], 2.0))
        tok_we = {}

        def emit_we_load(e):
            p.emit("gpsimd", lambda g, e=e:
                   g.dma_start(r_be[e % 2][:, :], d_be[e:e + 1, :]),
                   inc=f"we{e % 2}")
            tok_we[e] = p.emit("gpsimd", lambda g, e=e:
                               g.dma_start(wbig[e % 2][:, :].rearrange(
                                   "p (c f) -> p c f", c=DC),
                                   d_We[e, :, :].rearrange(
                                       "(c p) f -> p c f", p=128)),
                               inc=f"we{e % 2}")
        emit_we_load(0)
        emit_we_load(1)
        tok_wq = {}
        tok_wk = {}

        def emit_qk_load(hp):
            tok_wq[hp] = p.emit("gpsimd", lambda g, hp=hp:
                                g.dma_start(wqh[hp % 2][:, :].rearrange(
                                    "p (c f) -> p c f", c=DC),
                                    d_Wqkv[:, hp * 128:(hp + 1) * 128]
                                    .rearrange("(c p) f -> p c f", p=128)),
                                inc=f"qk{hp % 2}")
            tok_wk[hp] = p.emit("gpsimd", lambda g, hp=hp:
                                g.dma_start(wkh[hp % 2][:, :].rearrange(
                                    "p (c f) -> p c f", c=DC),
                                    d_Wqkv[:, D + hp * 128:D + (hp + 1) * 128]
                                    .rearrange("(c p) f -> p c f", p=128)),
                                inc=f"qk{hp % 2}")
        emit_qk_load(0)
        emit_qk_load(1)

        moe_slices = [(e, tc, nh) for e in range(NEXP) for tc in range(TC)
                      for nh in range(2)]
        moe_i = [0]
        moe_par = [0]
        tok_moe_ev = {}
        tok_moe_last_mm = {}

        def emit_moe_slice():
            if moe_i[0] >= len(moe_slices):
                return
            e, tc, nh = moe_slices[moe_i[0]]
            moe_i[0] += 1
            par = moe_par[0]
            moe_par[0] ^= 1
            out = P_moe[par]
            p.need("tensor", tok_we[e], tok_uT, tok_moe_ev.get(par))
            for dc in range(DC):
                p.emit("tensor", lambda t, out=out, dc=dc, tc=tc, e=e, nh=nh:
                       t.matmul(out,
                                uT[:, dc * T + tc * 128:dc * T + (tc + 1) * 128],
                                wbig[e % 2][:, dc * D + nh * 512:
                                            dc * D + (nh + 1) * 512],
                                start=(dc == 0), stop=False))
            tok_mm = p.emit("tensor", lambda t, out=out, e=e, nh=nh:
                            t.matmul(out, ones_r[0:1, 0:128],
                                     r_be[e % 2][0:1, nh * 512:(nh + 1) * 512],
                                     start=False, stop=True), inc="st")
            tok_moe_last_mm[e] = tok_mm
            p.need("vector", tok_mm)
            tok_moe_ev[par] = p.emit(
                "vector", lambda v, out=out, tc=tc, e=e, nh=nh:
                v.scalar_tensor_tensor(
                    xres[:, tc * D + nh * 512:tc * D + (nh + 1) * 512],
                    out, comb[:, tc * 8 + e:tc * 8 + e + 1],
                    xres[:, tc * D + nh * 512:tc * D + (nh + 1) * 512],
                    ALU.mult, ALU.add), inc="sv")
            if (tc, nh) == (TC - 1, 1) and e + 2 < NEXP:
                p.need("gpsimd", tok_mm)
                emit_we_load(e + 2)

        tok_psc_free = {0: None, 1: None}
        tok_pav_free = {0: [], 1: []}
        tok_pbc_free = None
        last_av = {}
        tok_qk_free = None
        tok_attn_dma = {}
        step = [0]

        for hp in range(H // 2):
            for qk in range(2):
                p.need("tensor", (tok_wq, tok_wk)[qk][hp], tok_uT, tok_qk_free)
                p.need("scalar", tok_bc)
                w = (wqh, wkh)[qk][hp % 2]
                dst = (q2T, k2T)[qk]
                tok_ev = {}
                for tq in range(4):
                    par = tq % 2
                    p.need("tensor", tok_psc_free[par], tok_ev.get(tq - 2))
                    out = P_sc[par]
                    for dc in range(DC):
                        p.emit("tensor", lambda t, out=out, w=w, dc=dc, tq=tq:
                               t.matmul(out, w[:, dc * 128:(dc + 1) * 128],
                                        uT[:, dc * T + tq * 512:
                                           dc * T + (tq + 1) * 512],
                                        start=(dc == 0), stop=(dc == DC - 1)),
                               inc="st" if dc == DC - 1 else None)
                    tok_m = ("st", p.cnt["st"])
                    p.need("scalar", tok_m)
                    tok_ev[tq] = p.emit(
                        "scalar", lambda s, dst=dst, par=par, tq=tq, qk=qk, hp=hp:
                        s.activation(dst[:, tq * 512:(tq + 1) * 512],
                                     P_sc[par], AF.Identity,
                                     bias=bqkv_cs[:, qk * 8 + hp:qk * 8 + hp + 1],
                                     scale=RS8), inc="ss")
                    tok_psc_free[par] = tok_ev[tq]
                    emit_moe_slice()
                tok_qkprod_last = ("st", p.cnt["st"])
            if hp + 2 < H // 2:
                p.need("gpsimd", tok_qkprod_last)
                emit_qk_load(hp + 2)
            tok_b = {}
            bi = 0
            for qk in range(2):
                src_t = (q2T, k2T)[qk]
                for half in range(4):
                    par = bi % 2
                    p.need("scalar", last_av.get(par))
                    tok_sq = p.emit(
                        "scalar", lambda s, src_t=src_t, half=half, par=par:
                        s.activation(ebuf[par][:, :],
                                     src_t[:, half * 512:(half + 1) * 512],
                                     AF.Square), inc="ss")
                    last_av[par] = tok_sq
                    p.need("tensor", tok_sq, *tok_pav_free[0])
                    p.need("tensor", tok_b.get(bi - 1))
                    p.emit("tensor", lambda t, par=par:
                           t.matmul(P_av[0][0:1, :], ones_c[:, 0:1],
                                    ebuf[par][:, :], start=True, stop=True),
                           inc="st")
                    tok_mmq = ("st", p.cnt["st"])
                    p.need("vector", tok_mmq)
                    tok_b[bi] = p.emit(
                        "vector", lambda v, bi=bi:
                        v.reduce_max(mqk[0:1, bi:bi + 1], P_av[0][0:1, :],
                                     axis=AX), inc="sv")
                    bi += 1
                    emit_moe_slice()
            p.need("scalar", tok_b[7])
            tok_ts = p.emit("scalar", lambda s:
                            s.activation(mqk[0:1, 8:16], mqk[0:1, 0:8],
                                         AF.Identity,
                                         accum_out=tsum[0:1, 0:1]), inc="ss")
            p.need("vector", tok_ts)
            tok_nb = p.emit("vector", lambda v:
                            v.tensor_scalar_mul(negb_bf[0:1, 0:1],
                                                tsum[0:1, 0:1], -0.5),
                            inc="sv")
            p.need("tensor", tok_nb, tok_pbc_free)
            tok_nbc = p.emit("tensor", lambda t:
                             t.matmul(P_nb, ones_r[0:1, 0:128],
                                      negb_bf[0:1, 0:1], start=True, stop=True),
                             inc="st")
            p.need("scalar", tok_nbc)
            tok_nbcol = p.emit("scalar", lambda s:
                               s.copy(negb_col[:, 0:1], P_nb), inc="ss")
            tok_pav_free[0] = [tok_b[7]]
            tok_pbc_free = tok_nbcol

            tok_sc_pair_last = None
            for hh in range(2):
                h = 2 * hp + hh
                rb = hh * 64
                for qq in range(4):
                    avp = qq % 2
                    tok_ex = {}
                    tok_av = {}
                    tok_vau = {}

                    def emit_scores(kc, rb=rb, qq=qq):
                        par = kc % 2
                        p.need("tensor", tok_psc_free[par], tok_nbcol)
                        p.emit("tensor", lambda t, rb=rb, kc=kc, qq=qq, par=par:
                               t.matmul(
                                   P_sc[par],
                                   k2T[rb:rb + 64, kc * 128:(kc + 1) * 128],
                                   q2T[rb:rb + 64, qq * 512:(qq + 1) * 512],
                                   start=True, stop=True), inc="st")
                        tok_sc = ("st", p.cnt["st"])
                        p.need("scalar", tok_sc, tok_nbcol, last_av.get(par))
                        tok_ex[kc] = p.emit(
                            "scalar", lambda s, kc=kc, par=par:
                            s.activation(ebuf[par][:, :], P_sc[par], AF.Exp,
                                         bias=negb_col[:, 0:1], scale=1.0),
                            inc="ss")
                        tok_psc_free[par] = tok_ex[kc]
                        p.need("vector", last_av.get(par))
                        tok_vau[kc] = p.emit(
                            "vector", lambda v, kc=kc, h=h, par=par:
                            v.tensor_copy(vaug[par][:, 0:64],
                                          vtok[:, kc * D + h * 64:
                                               kc * D + h * 64 + 64]),
                            inc="sv")
                        return tok_sc

                    def emit_av(kc, avp=avp):
                        par = kc % 2
                        p.need("tensor", tok_ex[kc], tok_vau[kc])
                        if kc == 0:
                            p.need("tensor", *tok_pav_free[avp])
                        p.emit("tensor", lambda t, kc=kc, avp=avp, par=par:
                               t.matmul(P_av[avp][0:65, :],
                                        vaug[par][:, 0:65],
                                        ebuf[par][:, :],
                                        start=(kc == 0), stop=(kc == 15)),
                               inc="st")
                        tok_av[kc] = ("st", p.cnt["st"])
                        last_av[par] = tok_av[kc]

                    for kc in range(16):
                        tok_sc_pair_last = emit_scores(kc)
                        if kc >= 1:
                            emit_av(kc - 1)
                        if moe_i[0] * 1024 < len(moe_slices) * step[0]:
                            emit_moe_slice()
                        step[0] += 1
                    emit_av(15)
                    p.need("vector", tok_av[15])
                    p.emit("vector", lambda v, avp=avp:
                           v.reciprocal(rzrow[0:1, 0:512],
                                        P_av[avp][64:65, :]))
                    tok_rzb = p.emit("vector", lambda v:
                                     v.tensor_copy(rzb[0:1, 0:512],
                                                   rzrow[0:1, 0:512]),
                                     inc="sv")
                    p.need("scalar", tok_av[15])
                    tok_ub = p.emit("scalar", lambda s, avp=avp:
                                    s.copy(attn_ub[0:64, 0:512],
                                           P_av[avp][0:64, :]), inc="ss")
                    p.need("tensor", tok_rzb, tok_pbc_free)
                    p.emit("tensor", lambda t:
                           t.matmul(P_bc[0:64, 0:512], ones_r[0:1, 0:64],
                                    rzb[0:1, 0:512], start=True, stop=True),
                           inc="st")
                    tok_bcm = ("st", p.cnt["st"])
                    p.need("vector", tok_bcm, tok_ub,
                           tok_attn_dma.get((h - 1, qq)))
                    tok_mult = p.emit(
                        "vector", lambda v, avp=avp:
                        v.tensor_mul(attn_n[avp][0:64, 0:512],
                                     attn_ub[0:64, 0:512], P_bc[0:64, 0:512]),
                        inc="sv")
                    tok_pav_free[avp] = [tok_rzb, tok_ub]
                    tok_pbc_free = tok_mult
                    p.need("sync", tok_mult)
                    tok_attn_dma[(h, qq)] = p.emit(
                        "sync", lambda y, h=h, qq=qq, avp=avp:
                        y.dma_start(
                            sc_att[h * 64:(h + 1) * 64,
                                   qq * 512:(qq + 1) * 512],
                            attn_n[avp][0:64, 0:512]), inc=f"ad{avp}")
            tok_qk_free = tok_sc_pair_last
        while moe_i[0] < len(moe_slices):
            emit_moe_slice()
        p.end()

        aes.close()
        tes = ExitStack()
        att_tm = [tes.enter_context(nc.sbuf_tensor(_uid(f"att_tm{i}"),
                                                   [128, D], BF16))
                  for i in range(2)]
        p.begin()
        tok_add = {}
        for tc in range(TC):
            p.need("sync", *[tok_attn_dma[(H - 1, q)] for q in range(4)])
            if tc >= 2:
                p.need("sync", tok_add[tc - 2])
            tok_tr = p.emit("sync", lambda y, tc=tc:
                            y.dma_start(att_tm[tc % 2][:, :],
                                        sc_att[:, tc * 128:(tc + 1) * 128],
                                        transpose=True), inc=f"at{tc % 2}")
            p.need("vector", tok_tr)
            tok_add[tc] = p.emit("vector", lambda v, tc=tc:
                                 v.tensor_add(xres[:, tc * D:(tc + 1) * D],
                                              xres[:, tc * D:(tc + 1) * D],
                                              att_tm[tc % 2][:, :]),
                                 inc="sv")
        p.end()
        tes.close()

    def ffn():
        tok_uT = layernorm_to_uT()
        gate_and_combine(tok_uT)
        ves = ExitStack()
        vtok = ves.enter_context(nc.sbuf_tensor(_uid("vtok"), [128, TC * D], BF16))
        wes = ExitStack()
        wv = wes.enter_context(nc.sbuf_tensor(_uid("wv"), [128, DC * D], BF16))
        p.begin()
        tok_wv = p.emit("gpsimd", lambda g:
                        g.dma_start(wv[:, :].rearrange("p (c f) -> p c f",
                                                       c=DC),
                                    d_Wqkv[:, 2 * D:3 * D].rearrange(
                                        "(c p) f -> p c f", p=128)),
                        inc="dg")
        tok_mm = {}
        tok_ev = {}
        for tc in range(TC):
            p.need("tensor", tok_wv, tok_uT, tok_bc)
            pt = (PA, PB)[tc % 2]
            if tc >= 4:
                p.need("tensor", tok_ev[tc - 4])
            for nh in range(2):
                out = pt[:, nh * 512:(nh + 1) * 512]
                for dc in range(DC):
                    p.emit("tensor", lambda t, out=out, dc=dc, tc=tc, nh=nh:
                           t.matmul(out,
                                    uT[:, dc * T + tc * 128:
                                       dc * T + (tc + 1) * 128],
                                    wv[:, dc * D + nh * 512:
                                       dc * D + (nh + 1) * 512],
                                    start=(dc == 0), stop=False))
                tok_mm[tc] = p.emit("tensor", lambda t, out=out, nh=nh:
                                    t.matmul(out, ones_r[0:1, 0:128],
                                             r_bqkv[0:1, nh * 512:
                                                    (nh + 1) * 512],
                                             start=False, stop=True),
                                    inc="st" if nh == 1 else None)
            p.need("scalar", tok_mm[tc])
            tok_ev[tc] = p.emit("scalar", lambda s, pt=pt, tc=tc:
                                s.copy(vtok[:, tc * D:(tc + 1) * D],
                                       pt[:, 0:1024]), inc="ss")
        p.end()
        wes.close()
        moe_attn(tok_uT, vtok)
        ves.close()

    def mingru(_dbg_done=[]):
        tok_uT = layernorm_to_uT()
        mes = ExitStack()
        lsb = lambda n, s, dt: mes.enter_context(nc.sbuf_tensor(_uid(n), s, dt))
        wkt = [lsb(f"wkt{i}", [128, 3 * DC * 128], BF16) for i in range(2)]
        fbuf = lsb("fbuf", [128, T], F32)
        vbuf = lsb("vbuf", [128, T], F32)
        kh05 = lsb("kh05", [128, T], F32)
        shb = lsb("shb", [128, T], F32)
        gbuf = lsb("gbuf", [128, T], F32)
        mskb = lsb("mskb", [128, T], U8)
        h_bf = lsb("h_bf", [128, T], BF16)

        p.begin()
        tok_wl = {}
        tok_mm = {}
        tok_fe = {}
        tok_ie = {}
        tok_he = {}
        tok_scan = {}
        tok_hdma = {}

        def emit_wkt_load(ec):
            p.need("gpsimd", tok_wt_done)
            for ci in range(3):
                tok_wl[ec] = p.emit(
                    "gpsimd", lambda g, ec=ec, ci=ci:
                    g.dma_start(
                        wkt[ec % 2][:, ci * DC * 128:(ci + 1) * DC * 128]
                        .rearrange("p (c e) -> p c e", c=DC),
                        sc_WT.rearrange("(c p) j -> p c j", p=128)[
                            :, :, ci * EMG + ec * 128:
                            ci * EMG + (ec + 1) * 128]),
                    inc=f"wk{ec % 2}")
        emit_wkt_load(0)
        emit_wkt_load(1)
        for ec in range(EC):
            p.need("tensor", tok_wl[ec], tok_uT)
            p.need("scalar", tok_b5v, tok_bc)
            for ci in range(3):
                pt = (PA, PB, PA)[ci]
                if ci == 0 and ec >= 1:
                    p.need("tensor", tok_he[ec - 1])
                if ci == 1 and ec >= 1:
                    p.need("tensor", tok_ie[ec - 1])
                if ci == 2:
                    p.need("tensor", tok_fe[ec])
                for tq in range(4):
                    out = pt[:, tq * 512:(tq + 1) * 512]
                    for dc in range(DC):
                        tok_mm[(ec, ci)] = p.emit(
                            "tensor",
                            lambda t, out=out, ec=ec, ci=ci, dc=dc, tq=tq:
                            t.matmul(out,
                                     wkt[ec % 2][:,
                                                 ci * DC * 128 + dc * 128:
                                                 ci * DC * 128 +
                                                 (dc + 1) * 128],
                                     uT[:, dc * T + tq * 512:
                                        dc * T + (tq + 1) * 512],
                                     start=(dc == 0), stop=(dc == DC - 1)),
                            inc="st" if (tq == 3 and dc == DC - 1) else None)
                p.need("scalar", tok_mm[(ec, ci)])
                if ec >= 1:
                    p.need("scalar", tok_scan[ec - 1])
                if ci == 0:
                    tok_fe[ec] = p.emit("scalar", lambda s, ec=ec:
                                        s.activation(fbuf[:, :], PA[:, :],
                                                     AF.Sigmoid,
                                                     bias=batt_cs[:, ec:ec + 1]),
                                        inc="ss")
                elif ci == 1:
                    tok_ie[ec] = p.emit("scalar", lambda s, ec=ec:
                                        s.activation(vbuf[:, :], PB[:, :],
                                                     AF.Sigmoid,
                                                     bias=batt_cs[:, 16 + ec:17 + ec]),
                                        inc="ss")
                else:
                    p.emit("scalar", lambda s, ec=ec:
                           s.activation(kh05[:, :], PA[:, :], AF.Identity,
                                        bias=bh05_cs[:, ec:ec + 1]))
                    tok_he[ec] = p.emit("scalar", lambda s, ec=ec:
                                        s.activation(shb[:, :], PA[:, :],
                                                     AF.Sigmoid,
                                                     bias=batt_cs[:, 32 + ec:33 + ec],
                                                     scale=1.0), inc="ss")
            p.need("vector", tok_he[ec], tok_ie[ec], tok_fe[ec])
            p.emit("vector", lambda v:
                   v.tensor_single_scalar(mskb[:, :], kh05[:, :], 0.5,
                                          ALU.is_ge))
            p.emit("vector", lambda v:
                   v.select(gbuf[:, :], mskb[:, :], kh05[:, :], shb[:, :]))
            p.emit("vector", lambda v:
                   v.tensor_mul(vbuf[:, :], vbuf[:, :], gbuf[:, :]))
            if ec >= 1:
                p.need("vector", tok_hdma[ec - 1])
            tok_scan[ec] = p.emit("vector", lambda v:
                                  v.tensor_tensor_scan(h_bf[:, :], fbuf[:, :],
                                                       vbuf[:, :], 0.5,
                                                       ALU.mult, ALU.add),
                                  inc="sv")
            p.need("sync", tok_scan[ec])
            tok_hdma[ec] = p.emit("sync", lambda y, ec=ec:
                                  y.dma_start(sc_h[ec, :, :], h_bf[:, :]),
                                  inc="dy")

            if ec + 2 < EC:
                p.need("gpsimd", tok_mm[(ec, 2)])
                emit_wkt_load(ec + 2)
        if DEBUG_TAPS and not _dbg_done:
            _dbg_done.append(1)
            p.need("sync", tok_hdma[EC - 1])
            p.emit("sync", lambda y: y.dma_start(
                dbg_h.rearrange("(e p) t -> e p t", p=128), sc_h[:, :, :]),
                inc="dy")
            p.emit("sync", lambda y: y.dma_start(dbg_w, sc_WT[:, :]),
                   inc="dy")
        p.end()
        mes.close()

        oes = ExitStack()
        wdt_s = oes.enter_context(nc.sbuf_tensor(_uid("wdt_s"), [128, EC * 512],
                                                 BF16))
        h_sl = [oes.enter_context(nc.sbuf_tensor(_uid(f"h_sl{i}"), [128, EC * 128],
                                                 BF16)) for i in range(2)]
        p.begin()
        tok_pmm = None
        for half in range(2):
            if half == 1:
                p.need("gpsimd", tok_pmm)
            p.need("gpsimd", tok_wdt_done)
            tok_wd = p.emit(
                "gpsimd", lambda g, half=half:
                g.dma_start(
                    wdt_s[:, :].rearrange("p (e d) -> p e d", e=EC),
                    sc_WdT.rearrange("(e p) d -> p e d", p=128)[
                        :, :, half * 512:(half + 1) * 512]), inc="wd0")
            tok_hl = {}
            if half == 0:
                tok_acc = {}
            for tc in range(TC):
                p.need("sync", tok_hdma[EC - 1])
                if tc >= 2:
                    p.need("sync", tok_acc[half * TC + tc - 2])
                elif half == 1:
                    p.need("sync", tok_acc[TC - 2 + tc])
                tok_hl[tc] = p.emit(
                    "sync", lambda y, tc=tc:
                    y.dma_start(h_sl[tc % 2][:, :].rearrange(
                        "p (e t) -> p e t", e=EC),
                        sc_h[:, :, tc * 128:(tc + 1) * 128].rearrange(
                            "e p t -> p e t")), inc=f"hs{tc % 2}")
                p.need("tensor", tok_wd, tok_hl[tc])
                if tc >= 2:
                    p.need("tensor", tok_acc[half * TC + tc - 2])
                pt = (PA, PB)[tc % 2]
                for ec in range(EC):
                    tok_pmm = p.emit(
                        "tensor", lambda t, pt=pt, ec=ec, tc=tc:
                        t.matmul(pt[:, 0:512],
                                 h_sl[tc % 2][:, ec * 128:(ec + 1) * 128],
                                 wdt_s[:, ec * 512:(ec + 1) * 512],
                                 start=(ec == 0), stop=(ec == EC - 1)),
                        inc="st" if ec == EC - 1 else None)
                p.need("vector", tok_pmm)
                tok_acc[half * TC + tc] = p.emit(
                    "vector", lambda v, pt=pt, tc=tc, half=half:
                    v.tensor_add(
                        xres[:, tc * D + half * 512:
                             tc * D + (half + 1) * 512],
                        xres[:, tc * D + half * 512:
                             tc * D + (half + 1) * 512],
                        pt[:, 0:512]), inc="sv")
        p.end()
        oes.close()

    for seq in range(SPC):
        p.begin()
        tok_x = p.emit("sync", lambda y, seq=seq:
                       y.dma_start(xres[:, :].rearrange("p (tc d) -> p tc d", tc=TC),
                                   d_x[seq * T:(seq + 1) * T, :].rearrange(
                                       "(tc p) d -> p tc d", p=128)),
                       inc="dy")
        p.need("sync", tok_x)
        p.end()

        ffn()
        if STAGE >= 2:
            mingru()
        if STAGE >= 3:
            ffn()

        p.begin()
        tok_y = p.emit("sync", lambda y, seq=seq:
                       y.dma_start(d_y[seq * T:(seq + 1) * T, :].rearrange(
                           "(tc p) d -> p tc d", p=128),
                           xres[:, :].rearrange("p (tc d) -> p tc d", tc=TC)),
                       inc="dy")
        p.need("sync", tok_y)
        p.end()

    es.close()
    nc.reset()
    return nc


_NC_CACHE = None


def kernel(L0_attn_x, L0_attn_W, L0_attn_b, L0_attn_Wd, L0_ffn_W_qkv,
           L0_ffn_b_qkv, L0_ffn_W_gate, L0_ffn_b_gate, L0_ffn_W_experts,
           L0_ffn_b_experts):
    global _NC_CACHE
    x = np.ascontiguousarray(np.asarray(L0_attn_x, np.float32))
    B = x.shape[0]
    xf = x.reshape(B * T, D)
    if _NC_CACHE is None:
        _NC_CACHE = build_program()
    nc = _NC_CACHE
    common = {
        "W": np.ascontiguousarray(np.asarray(L0_attn_W, np.float32)),
        "b": np.ascontiguousarray(np.asarray(L0_attn_b, np.float32)).reshape(1, E3),
        "Wd": np.ascontiguousarray(np.asarray(L0_attn_Wd, np.float32)),
        "Wqkv": np.ascontiguousarray(np.asarray(L0_ffn_W_qkv, np.float32)),
        "bqkv": np.ascontiguousarray(np.asarray(L0_ffn_b_qkv, np.float32)).reshape(1, 3 * D),
        "Wg": np.ascontiguousarray(np.asarray(L0_ffn_W_gate, np.float32)),
        "bg": np.ascontiguousarray(np.asarray(L0_ffn_b_gate, np.float32)).reshape(1, NEXP),
        "We": np.ascontiguousarray(np.asarray(L0_ffn_W_experts, np.float32)),
        "be": np.ascontiguousarray(np.asarray(L0_ffn_b_experts, np.float32)),
    }
    in_maps = []
    for c in range(NCORES):
        m = dict(common)
        m["x"] = np.ascontiguousarray(xf[c * SPC * T:(c + 1) * SPC * T, :])
        in_maps.append(m)
    res = run_bass_kernel_spmd(nc, in_maps, core_ids=list(range(NCORES)))
    y = np.concatenate([res.results[c]["y"] for c in range(NCORES)], axis=0)
    return y.reshape(B, T, D)
